# version 03
# baseline (speedup 1.0000x reference)
import math
import numpy as np
from contextlib import ExitStack

import concourse.bass as bass
import concourse.mybir as mybir
from concourse.bass_utils import run_bass_kernel_spmd

F32 = mybir.dt.float32
BF16 = mybir.dt.bfloat16
AF = mybir.ActivationFunctionType
ALU = mybir.AluOpType
AX = mybir.AxisListType

NCORES = 8
NSEQ = 4
T = 2048
D = 1024
NH = 16
MAC = 512
NMAC = T // MAC
GRP = 1024
NGRP = NSEQ * T // GRP
NE = 6
ISQ = 1.0 / math.sqrt(96.0)
CEXP = 12.0


def _rope_scale():
    t = np.arange(T, dtype=np.float32)
    inv = 1.0 / 10000.0 ** (np.arange(0, 32, 2, dtype=np.float32) / 32)
    freqs = t[:, None] * inv[None, :]
    emb = np.concatenate([freqs, freqs], -1)
    return (np.cos(emb) + np.sin(emb)).astype(np.float32)


class Ch:
    def __init__(self, nc, ctx, name):
        self.h = ctx.enter_context(nc.semaphore(name))
        self.n = 0


class Emit:
    def __init__(self):
        self.ops = {k: [] for k in "tvsyg"}

    def op(self, key, waits, build, inc=None, k=1):
        for (c, n) in waits:
            assert n <= c.n, f"wait on future count {n} > {c.n} ({key})"
        toks = [(c.h, n) for (c, n) in waits if n > 0]
        if inc is not None:
            inc.n += k
            semh = inc.h
        else:
            semh = None

        def f(e, toks=toks, build=build, semh=semh, k=k):
            for (h, n) in toks:
                e.wait_ge(h, n)
            r = build(e)
            if semh is not None:
                r.then_inc(semh, k)
        self.ops[key].append(f)


def build_program(dbg=False):
    nc = bass.Bass("TRN2", target_bir_lowering=False)
    EM = Emit()
    ctx = ExitStack()

    def din(name, shape, dt=F32):
        return nc.dram_tensor(name, list(shape), dt, kind="ExternalInput").ap()

    X = din("x", (NSEQ, T, D))
    W_DQ = din("W_DQ", (1024, 4096))
    W_UQ = din("W_UQ", (4096, 1024))
    W_QR = din("W_QR", (4096, 512))
    W_DKV = din("W_DKV", (1024, 256))
    W_UK = din("W_UK", (256, 1024))
    W_UV = din("W_UV", (256, 1024))
    W_KR2 = din("W_KR2", (1024, 64))
    W_O_pair = din("W_O_pair", (128, 8, 1024))
    RW = din("router_W", (2, 1024, NE))
    RB = din("router_b", (2, NE))
    GB = din("gate_bias", (2, NE))
    EWUP = din("exp_Wup", (NE, 1024, 1024))
    EBUP = din("exp_bup", (NE, 1024))
    EWG = din("exp_Wg", (NE, 1024, 1024))
    EBG = din("exp_bg", (NE, 1024))
    EWDN = din("exp_Wdown", (NE, 1024, 1024))
    EBDN = din("exp_bdown", (NE, 1024))
    SWUP = din("share_Wup", (1024, 2048))
    SBUP = din("share_bup", (2048,))
    SWG = din("share_Wg", (1024, 1024))
    SBG = din("share_bg", (1024,))
    SWDN = din("share_Wdown", (1024, 1024))
    SBDN = din("share_bdown", (1024,))
    NW = din("norm_w", (2, 1024))
    C_I = din("CONST_I", (128, 128))
    C_TRI = din("CONST_TRI", (128, 896))
    C_SC2 = din("CONST_SC2", (64, T))
    C_SEL = din("CONST_SEL", (8, 1024))

    OUT = nc.dram_tensor("out", [NSEQ, T, D], F32, kind="ExternalOutput").ap()

    def dscr(name, shape, dt):
        return nc.dram_tensor(name, list(shape), dt, kind="Internal").ap()

    WDQ16d = dscr("WDQ16d", (1024, 4096), BF16)
    WC16 = dscr("WC16", (22, 1024, 1024), BF16)
    _k = "ExternalOutput" if dbg else "Internal"
    X2S = nc.dram_tensor("X2S", [NSEQ * T // 128, 128, D], F32, kind=_k).ap()
    XF0S = nc.dram_tensor("XF0S", [NSEQ * T // 128, 128, D], F32, kind=_k).ap()

    if dbg:
        DBGA = nc.dram_tensor("DBGA", [128, 8192], BF16, kind="ExternalOutput").ap()
        DBGB = nc.dram_tensor("DBGB", [128, 4096], F32, kind="ExternalOutput").ap()
        DBGC = nc.dram_tensor("DBGC", [128, 4096], BF16, kind="ExternalOutput").ap()
        DBGD = nc.dram_tensor("DBGD", [128, 4096], BF16, kind="ExternalOutput").ap()
        DBGE = nc.dram_tensor("DBGE", [128, 4096], BF16, kind="ExternalOutput").ap()
        DBGF = nc.dram_tensor("DBGF", [128, 4096], BF16, kind="ExternalOutput").ap()
        DBGH = nc.dram_tensor("DBGH", [128, 8192], BF16, kind="ExternalOutput").ap()
        DBGI = nc.dram_tensor("DBGI", [128, 64], F32, kind="ExternalOutput").ap()
        DBGJ = nc.dram_tensor("DBGJ", [128, 8192], F32, kind="ExternalOutput").ap()
        DBGK = nc.dram_tensor("DBGK", [128, 8192], F32, kind="ExternalOutput").ap()
        DBGL = nc.dram_tensor("DBGL", [128, 8192], F32, kind="ExternalOutput").ap()
        DBGG = nc.dram_tensor("DBGG", [128, 4096], BF16, kind="ExternalOutput").ap()
    cT = Ch(nc, ctx, "cT")
    cV = Ch(nc, ctx, "cV")
    cS = Ch(nc, ctx, "cS")
    cY = Ch(nc, ctx, "cY")
    cG = Ch(nc, ctx, "cG")

    def W(ch):
        return (ch, ch.n)

    def barrier(keys="tvs"):
        snap = [(c, c.n) for c in (cT, cV, cS, cY, cG) if c.n > 0]
        for k in keys:
            EM.op(k, snap, lambda e: e.nop())

    def sb(name, shape, dt):
        return ctx.enter_context(nc.sbuf_tensor(name, list(shape), dt))

    cst = sb("cst", (128, 8), F32)
    ones16 = sb("ones16", (128, 128), BF16)
    onesf = sb("onesf", (128, 128), F32)
    I128f = sb("I128f", (128, 128), F32)
    I128b = sb("I128b", (128, 128), BF16)
    TRI16 = sb("TRI16", (128, 896), BF16)
    SC2_16 = sb("SC2_16", (64, T), BF16)
    SEL16 = sb("SEL16", (8, 1024), BF16)
    rW16 = sb("rW16", (128, 2 * 8 * NE), BF16)
    rbgb16 = sb("rbgb16", (1, 24), BF16)
    gbbc = sb("gbbc", (128, 16), F32)
    bup_st = sb("bup_st", (128, 48), F32)
    bg_st = sb("bg_st", (128, 48), F32)
    bdn_st = sb("bdn_st", (128, 48), F32)
    sbup_st = sb("sbup_st", (128, 16), F32)
    sbg_st = sb("sbg_st", (128, 8), F32)
    sbdn_st = sb("sbdn_st", (128, 8), F32)
    nw_st = sb("nw_st", (128, 16), F32)
    bstage = sb("bstage", (48, 128), F32)

    nc.const_aps.register(F32, -CEXP, cst[:, 0:1])
    nc.const_aps.register(F32, 1e-5, cst[:, 1:2])
    nc.const_aps.register(F32, 1e-6, cst[:, 2:3])

    EM.op("v", [], lambda e: e.memset(cst[:, 0:1], -CEXP))
    EM.op("v", [], lambda e: e.memset(cst[:, 1:2], 1e-5))
    EM.op("v", [], lambda e: e.memset(cst[:, 2:3], 1e-6))
    EM.op("v", [], lambda e: e.memset(ones16[:, :], 1.0))
    EM.op("v", [], lambda e: e.memset(onesf[:, :], 1.0), inc=cV)

    EM.op("y", [], lambda e: e.dma_start(out=I128f[:, :], in_=C_I), inc=cY, k=16)
    cy_i = cY.n
    EM.op("v", [(cY, cy_i)], lambda e: e.tensor_copy(I128b[:, :], I128f[:, :]), inc=cV)
    cv_i128b = cV.n

    EM.op("g", [], lambda e: e.dma_start(out=TRI16[:, :], in_=C_TRI), inc=cG, k=16)
    EM.op("g", [], lambda e: e.dma_start(out=SC2_16[:, :], in_=C_SC2), inc=cG, k=16)
    EM.op("g", [], lambda e: e.dma_start(out=SEL16[:, :], in_=C_SEL), inc=cG, k=16)
    EM.op("g", [], lambda e: e.dma_start(
        out=rW16.rearrange("p (j cb e) -> p j cb e", j=2, cb=8),
        in_=RW.rearrange("j (cb p) e -> p j cb e", p=128)), inc=cG, k=16)
    EM.op("g", [], lambda e: e.dma_start(
        out=rbgb16[0:1, 0:12], in_=RB.rearrange("j e -> (j e)")[None, :]), inc=cG, k=16)
    EM.op("g", [], lambda e: e.dma_start(
        out=rbgb16[0:1, 12:24], in_=GB.rearrange("j e -> (j e)")[None, :]), inc=cG, k=16)
    cg_small = cG.n

    ao = ExitStack()

    def aosb(name, shape, dt):
        return ao.enter_context(nc.sbuf_tensor(name, list(shape), dt))

    Wq16 = aosb("Wq16", (128, 8192), BF16)
    Wr16 = aosb("Wr16", (128, 4096), BF16)
    Wdkv16 = aosb("Wdkv16", (128, 8 * 256), BF16)
    Wuk16 = aosb("Wuk16", (128, 2 * 1024), BF16)
    Wuv16 = aosb("Wuv16", (128, 2 * 1024), BF16)
    Wkr2_16 = aosb("Wkr2_16", (128, 8 * 64), BF16)
    Wo16 = aosb("Wo16", (128, 8 * 1024), BF16)

    EM.op("g", [], lambda e: e.dma_start(
        out=Wdkv16.rearrange("p (c o) -> p c o", c=8), in_=W_DKV.rearrange("(c p) o -> p c o", p=128)), inc=cG, k=16)
    EM.op("g", [], lambda e: e.dma_start(
        out=Wuk16.rearrange("p (c o) -> p c o", c=2), in_=W_UK.rearrange("(c p) o -> p c o", p=128)), inc=cG, k=16)
    EM.op("g", [], lambda e: e.dma_start(
        out=Wuv16.rearrange("p (c o) -> p c o", c=2), in_=W_UV.rearrange("(c p) o -> p c o", p=128)), inc=cG, k=16)
    EM.op("g", [], lambda e: e.dma_start(
        out=Wkr2_16.rearrange("p (c o) -> p c o", c=8), in_=W_KR2.rearrange("(c p) o -> p c o", p=128)), inc=cG, k=16)
    EM.op("g", [], lambda e: e.dma_start(
        out=Wo16[:, :], in_=W_O_pair.rearrange("p q o -> p (q o)")), inc=cG, k=16)
    cg_attnw = cG.n

    p0 = ExitStack()
    psB0 = p0.enter_context(nc.psum_tensor("psB0", [128, 512], F32))
    psB1 = p0.enter_context(nc.psum_tensor("psB1", [128, 512], F32))
    bias_specs = [
        (bup_st, EBUP, "e (c p) -> (e c) p", 48),
        (bg_st, EBG, "e (c p) -> (e c) p", 48),
        (bdn_st, EBDN, "e (c p) -> (e c) p", 48),
        (sbup_st, SBUP, "(c p) -> c p", 16),
        (sbg_st, SBG, "(c p) -> c p", 8),
        (sbdn_st, SBDN, "(c p) -> c p", 8),
        (nw_st, NW, "j (c p) -> (j c) p", 16),
    ]
    cv_prev_bias = 0
    for i, (dst, src, pat, rows) in enumerate(bias_specs):
        EM.op("y", ([(cV, cv_prev_bias)] if i > 0 else []) + [],
              lambda e, src=src, pat=pat, rows=rows:
              e.dma_start(out=bstage[0:rows, 0:128], in_=src.rearrange(pat, p=128)),
              inc=cY, k=16)
        EM.op("t", [W(cY), (cY, cy_i), (cV, cv_prev_bias)],
              lambda e, rows=rows:
              e.transpose(psB0[:, 0:rows], bstage[0:rows, 0:128], I128f[0:rows, 0:rows]),
              inc=cT)
        EM.op("v", [W(cT)], lambda e, dst=dst, rows=rows:
              e.tensor_copy(dst[:, 0:rows], psB0[:, 0:rows]), inc=cV)
        cv_prev_bias = cV.n
    for j in range(2):
        EM.op("t", [(cG, cg_small), W(cV)], lambda e, j=j:
              e.matmul(psB1[:, 0:6], ones16[0:1, 0:128], rbgb16[0:1, 12 + 6 * j:18 + 6 * j],
                       start=True, stop=True), inc=cT)
        EM.op("v", [W(cT)], lambda e, j=j:
              e.tensor_copy(gbbc[:, 8 * j:8 * j + 6], psB1[:, 0:6]), inc=cV)
    cv_bias_done = cV.n

    EM.op("g", [], lambda e: e.dma_start(out=WDQ16d, in_=W_DQ), inc=cG, k=16)
    cg_wdq = cG.n

    wdqT = p0.enter_context(nc.sbuf_tensor("wdqT", [128, 32 * 1024], BF16))
    wbig = p0.enter_context(nc.sbuf_tensor("wbig", [128, 32 * 1024], BF16))

    EM.op("g", [], lambda e: e.dma_start(
        out=wbig.rearrange("p (c o) -> p c o", c=32), in_=W_UQ.rearrange("(cb p) o -> p cb o", p=128)), inc=cG, k=16)
    cg_wuq = cG.n

    for cb in range(32):
        EM.op("y", [(cG, cg_wdq)] if cb == 0 else [], lambda e, cb=cb: e.dma_start(
            out=wdqT[:, 1024 * cb:1024 * (cb + 1)], in_=WDQ16d[:, 128 * cb:128 * (cb + 1)],
            transpose=True), inc=cY, k=16)
    cy_wdqT = cY.n

    if dbg:
        EM.op("y", [], lambda e: e.dma_start(out=DBGF, in_=wdqT[:, 0:4096]), inc=cY, k=16)
        EM.op("y", [(cG, cg_wuq)], lambda e: e.dma_start(out=DBGG, in_=wbig[:, 0:4096]), inc=cY, k=16)
    psF0 = p0.enter_context(nc.psum_tensor("psF0", [128, 512], F32))
    psF1 = p0.enter_context(nc.psum_tensor("psF1", [128, 512], F32))
    PSF = [psF0, psF1]
    fold_ev = {0: 0, 1: 0}
    for ic in range(8):
        for half in range(2):
            slot = (ic * 2 + half) % 2
            ps = PSF[slot]
            waits = [(cY, cy_wdqT), (cG, cg_wuq), (cV, fold_ev[slot])]
            for cb in range(32):
                EM.op("t", waits if cb == 0 else [],
                      lambda e, ps=ps, ic=ic, half=half, cb=cb: e.matmul(
                          ps[:, :], wdqT[:, 1024 * cb + 128 * ic:1024 * cb + 128 * ic + 128],
                          wbig[:, 1024 * cb + 512 * half:1024 * cb + 512 * half + 512],
                          start=(cb == 0), stop=(cb == 31)),
                      inc=cT if cb == 31 else None)
            EM.op("v", [W(cT)], lambda e, ps=ps, ic=ic, half=half: e.tensor_copy(
                Wq16[:, 1024 * ic + 512 * half:1024 * ic + 512 * half + 512], ps[:, :]), inc=cV)
            fold_ev[slot] = cV.n
    EM.op("g", [W(cT)], lambda e: e.dma_start(
        out=wbig[:, 0:32 * 512].rearrange("p (c o) -> p c o", c=32), in_=W_QR.rearrange("(cb p) o -> p cb o", p=128)), inc=cG, k=16)
    cg_wqr = cG.n
    for ic in range(8):
        slot = ic % 2
        ps = PSF[slot]
        waits = [(cG, cg_wqr), (cV, fold_ev[slot])]
        for cb in range(32):
            EM.op("t", waits if cb == 0 else [],
                  lambda e, ps=ps, ic=ic, cb=cb: e.matmul(
                      ps[:, :], wdqT[:, 1024 * cb + 128 * ic:1024 * cb + 128 * ic + 128],
                      wbig[:, 512 * cb:512 * cb + 512],
                      start=(cb == 0), stop=(cb == 31)),
                  inc=cT if cb == 31 else None)
        EM.op("v", [W(cT)], lambda e, ps=ps, ic=ic: e.tensor_copy(
            Wr16[:, 512 * ic:512 * ic + 512], ps[:, :]), inc=cV)
        fold_ev[slot] = cV.n
    cv_fold_done = cV.n

    if dbg:
        EM.op("y", [(cV, cv_fold_done)], lambda e: e.dma_start(out=DBGH, in_=Wq16[:, :]), inc=cY, k=16)
    cache_srcs = [SWUP[:, 0:1024], SWG, SWDN, SWUP[:, 1024:2048]]
    for e_ in range(NE):
        cache_srcs += [EWUP[e_], EWG[e_], EWDN[e_]]
    wc_done_at = []
    for mi, srcap in enumerate(cache_srcs):
        EM.op("g", [], lambda e, mi=mi, srcap=srcap: e.dma_start(
            out=WC16[mi], in_=srcap), inc=cG, k=16)
        wc_done_at.append(cG.n)

    p0.close()

    at = ExitStack()

    def asb(name, shape, dt):
        return at.enter_context(nc.sbuf_tensor(name, list(shape), dt))

    kst16 = asb("kst16", (128, 8 * T), BF16)
    kr2st16 = asb("kr2st16", (64, T), BF16)
    vst16 = asb("vst16", (128, 16 * 1024), BF16)
    x2tm = asb("x2tm", (128, 4 * 1024), F32)
    sqsc = asb("sqsc", (128, 1024), F32)
    lntmp = asb("lntmp", (128, 2 * 1024), BF16)
    ln1_16 = asb("ln1_16", (128, 8 * MAC), BF16)
    q16 = asb("q16", (128, 8 * MAC), BF16)
    qr16 = asb("qr16", (64, 8 * MAC), BF16)
    ckv16 = asb("ckv16", (128, 2 * MAC), BF16)
    E16 = asb("E16", (128, 2 * MAC), BF16)
    ypair16 = asb("ypair16", (128, 8 * MAC), BF16)
    fpool = asb("fpool", (128, 3 * 512), F32)
    xf0st = asb("xf0st", (128, 2 * 1024), F32)
    statc = asb("statc", (128, 8), F32)

    psA = at.enter_context(nc.psum_tensor("psA", [128, 512], F32))
    psPB = at.enter_context(nc.psum_tensor("psPB", [128, 512], F32))
    psO0 = at.enter_context(nc.psum_tensor("psO0", [128, 512], F32))
    psO1 = at.enter_context(nc.psum_tensor("psO1", [128, 512], F32))
    psL0 = at.enter_context(nc.psum_tensor("psL0", [128, 512], F32))
    psL1 = at.enter_context(nc.psum_tensor("psL1", [128, 512], F32))
    psTB = at.enter_context(nc.psum_tensor("psTB", [128, 1024], BF16))
    PS2 = [psA, psPB]

    EM.op("t", [(cV, cv_fold_done), (cG, cg_attnw), (cV, cv_bias_done), (cG, cg_small)],
          lambda e: e.nop(), inc=cT)

    sq_state = {"s": 0}

    def ln_stats_apply(u_slice_f32, out_ap):
        EM.op("v", [W(cS), W(cV)], lambda e: e.reduce_sum(statc[:, 0:1], u_slice_f32, axis=AX.X), inc=cV)
        EM.op("s", [W(cV), (cS, sq_state["s"])],
              lambda e: e.activation(sqsc[:, :], u_slice_f32, AF.Square), inc=cS)
        EM.op("v", [W(cS)], lambda e: e.reduce_sum(statc[:, 1:2], sqsc[:, :], axis=AX.X), inc=cV)
        sq_state["s"] = cS.n
        EM.op("v", [W(cV)], lambda e: e.tensor_scalar_mul(statc[:, 2:3], statc[:, 0:1], -1.0 / 1024.0), inc=cV)
        EM.op("v", [W(cV)], lambda e: e.tensor_tensor(statc[:, 3:4], statc[:, 2:3], statc[:, 2:3], ALU.mult), inc=cV)
        EM.op("v", [W(cV)], lambda e: e.tensor_scalar_mul(statc[:, 5:6], statc[:, 1:2], 1.0 / 1024.0), inc=cV)
        EM.op("v", [W(cV)], lambda e: e.tensor_sub(statc[:, 4:5], statc[:, 5:6], statc[:, 3:4]), inc=cV)
        EM.op("s", [W(cV)], lambda e: e.activation(statc[:, 6:7], statc[:, 4:5], AF.Ln, bias=1e-5), inc=cS)
        EM.op("s", [W(cS)], lambda e: e.activation(statc[:, 7:8], statc[:, 6:7], AF.Exp, scale=-0.5), inc=cS)
        EM.op("v", [W(cS)], lambda e: e.tensor_scalar(
            out_ap, u_slice_f32, statc[:, 2:3], statc[:, 7:8], ALU.add, ALU.mult), inc=cV)

    xf0_dma_y = [0, 0]
    x2tm_free = [(cV, cv_fold_done), (cV, cv_bias_done)]
    for s in range(NSEQ):
        for m in range(NMAC):
            mt0 = MAC * m
            EM.op("y", [x2tm_free[0], x2tm_free[1]], lambda e, s=s, m=m: e.dma_start(
                out=x2tm.rearrange("p (u c) -> p u c", u=4),
                in_=X[s, MAC * m:MAC * (m + 1), :].rearrange("(u p) c -> p u c", p=128)),
                inc=cY, k=16)
            cy_x = cY.n

            if dbg and s == 0 and m == 0:
                EM.op("y", [(cY, cy_x)], lambda e: e.dma_start(out=DBGB, in_=x2tm[:, :]), inc=cY, k=16)
            psTB_free = cV.n
            for u in range(4):
                slot = u % 2
                lt = lntmp[:, 1024 * slot:1024 * (slot + 1)]
                EM.op("v", [(cY, cy_x)], lambda e: e.nop())
                ln_stats_apply(x2tm[:, 1024 * u:1024 * (u + 1)], lt)
                cv_ln = cV.n
                for c in range(8):
                    EM.op("t", [(cV, cv_ln), (cV, psTB_free), (cV, cv_i128b)] if c == 0 else [],
                          lambda e, lt=lt, c=c: e.transpose(
                              psTB[:, 128 * c:128 * (c + 1)], lt[:, 128 * c:128 * (c + 1)],
                              I128b[:, :]),
                          inc=cT if c == 7 else None)
                EM.op("v", [W(cT)], lambda e, u=u: e.tensor_copy(
                    ln1_16.rearrange("p (c t) -> p c t", c=8)[:, :, 128 * u:128 * (u + 1)],
                    psTB.rearrange("p (c t) -> p c t", c=8)[:, :, :]), inc=cV)
                psTB_free = cV.n
            cv_ln1 = cV.n

            if dbg and s == 0 and m == 0:
                EM.op("y", [(cV, cv_ln1)], lambda e: e.dma_start(out=DBGC, in_=ln1_16[:, :]), inc=cY, k=16)
                EM.op("y", [(cV, cv_fold_done)], lambda e: e.dma_start(out=DBGA, in_=Wq16[:, :]), inc=cY, k=16)
            ps_free = {0: cV.n, 1: cV.n}
            proj_jobs = ([("q", co) for co in range(8)] + [("qr", jj) for jj in range(8)]
                         + [("ckv", db) for db in range(2)] + [("kc", co) for co in range(8)]
                         + [("v", sv) for sv in range(8)] + [("kr2", 0)])
            ckv_ready = 0
            for ji, (kind, idx) in enumerate(proj_jobs):
                slot = ji % 2
                ps = PS2[slot]
                w0 = [(cV, ps_free[slot])]
                if kind in ("q", "qr", "ckv", "kr2"):
                    w0.append((cV, cv_ln1))
                if kind in ("kc", "v"):
                    w0.append((cV, ckv_ready))
                if kind == "q":
                    for ci in range(8):
                        EM.op("t", w0 if ci == 0 else [], lambda e, ps=ps, co=idx, ci=ci: e.matmul(
                            ps[:, :], Wq16[:, 1024 * ci + 128 * co:1024 * ci + 128 * co + 128],
                            ln1_16[:, 512 * ci:512 * ci + 512],
                            start=(ci == 0), stop=(ci == 7)), inc=cT if ci == 7 else None)
                    EM.op("v", [W(cT)], lambda e, ps=ps, co=idx: e.tensor_copy(
                        q16[:, 512 * co:512 * co + 512], ps[:, :]), inc=cV)
                elif kind == "qr":
                    for ci in range(8):
                        EM.op("t", w0 if ci == 0 else [], lambda e, ps=ps, jj=idx, ci=ci: e.matmul(
                            ps[0:64, :], Wr16[:, 512 * ci + 64 * jj:512 * ci + 64 * jj + 64],
                            ln1_16[:, 512 * ci:512 * ci + 512],
                            start=(ci == 0), stop=(ci == 7)), inc=cT if ci == 7 else None)
                    EM.op("v", [W(cT)], lambda e, ps=ps, jj=idx, mt0=mt0: e.tensor_tensor(
                        qr16[0:64, 512 * jj:512 * jj + 512], ps[0:64, :],
                        SC2_16[0:64, mt0:mt0 + 512], ALU.mult), inc=cV)
                elif kind == "ckv":
                    for ci in range(8):
                        EM.op("t", w0 if ci == 0 else [], lambda e, ps=ps, db=idx, ci=ci: e.matmul(
                            ps[:, :], Wdkv16[:, 256 * ci + 128 * db:256 * ci + 128 * db + 128],
                            ln1_16[:, 512 * ci:512 * ci + 512],
                            start=(ci == 0), stop=(ci == 7)), inc=cT if ci == 7 else None)
                    EM.op("v", [W(cT)], lambda e, ps=ps, db=idx: e.tensor_copy(
                        ckv16[:, 512 * db:512 * db + 512], ps[:, :]), inc=cV)
                    if idx == 1:
                        ckv_ready = cV.n
                elif kind == "kc":
                    for db in range(2):
                        EM.op("t", w0 if db == 0 else [], lambda e, ps=ps, co=idx, db=db: e.matmul(
                            ps[:, :], Wuk16[:, 1024 * db + 128 * co:1024 * db + 128 * co + 128],
                            ckv16[:, 512 * db:512 * db + 512],
                            start=(db == 0), stop=(db == 1)), inc=cT if db == 1 else None)
                    EM.op("v", [W(cT)], lambda e, ps=ps, co=idx, m=m: e.tensor_copy(
                        kst16[:, 2048 * co + MAC * m:2048 * co + MAC * m + 512], ps[:, :]), inc=cV)
                elif kind == "v":
                    s2, half = idx // 2, idx % 2
                    for db in range(2):
                        EM.op("t", w0 if db == 0 else [], lambda e, ps=ps, s2=s2, half=half, db=db: e.matmul(
                            ps[:, :], ckv16[:, 512 * db + 128 * s2:512 * db + 128 * s2 + 128],
                            Wuv16[:, 1024 * db + 512 * half:1024 * db + 512 * half + 512],
                            start=(db == 0), stop=(db == 1)), inc=cT if db == 1 else None)
                    EM.op("v", [W(cT)], lambda e, ps=ps, s2=s2, half=half, m=m: e.tensor_copy(
                        vst16[:, 1024 * (4 * m + s2) + 512 * half:1024 * (4 * m + s2) + 512 * half + 512],
                        ps[:, :]), inc=cV)
                elif kind == "kr2":
                    for ci in range(8):
                        EM.op("t", w0 if ci == 0 else [], lambda e, ps=ps, ci=ci: e.matmul(
                            ps[0:64, :], Wkr2_16[:, 64 * ci:64 * ci + 64],
                            ln1_16[:, 512 * ci:512 * ci + 512],
                            start=(ci == 0), stop=(ci == 7)), inc=cT if ci == 7 else None)
                    EM.op("v", [W(cT)], lambda e, ps=ps, mt0=mt0: e.tensor_tensor(
                        kr2st16[0:64, mt0:mt0 + 512], ps[0:64, :],
                        SC2_16[0:64, mt0:mt0 + 512], ALU.mult), inc=cV)
                ps_free[slot] = cV.n
            cv_proj = cV.n

            if dbg and s == 0 and m == 0:
                EM.op("y", [(cV, cv_proj)], lambda e: e.dma_start(out=DBGD, in_=q16[:, :]), inc=cY, k=16)
            nkt = 4 * (m + 1)
            sS_free = {0: cS.n, 1: cS.n}
            E_free = {0: cT.n, 1: cT.n}
            O_free = {0: cV.n, 1: cV.n}
            L_free = {0: cV.n, 1: cV.n}
            for h in range(NH):
                oslot = h % 2
                psO = psO0 if oslot == 0 else psO1
                psL = psL0 if oslot == 0 else psL1
                b = 0 if h < 8 else 64
                hp = h % 8
                rbase = 64 * (h % 2)
                rb32 = 32 * (h % 2)
                for j in range(nkt):
                    ssl = j % 2
                    psS = PS2[ssl]
                    esl = j % 2
                    w1 = [(cS, sS_free[ssl])]
                    if h == 0:
                        w1.append((cV, cv_proj))
                    if j == 0:
                        w1 += [(cV, O_free[oslot]), (cV, L_free[oslot])]
                    EM.op("t", w1, lambda e, psS=psS, h=h, j=j, rbase=rbase: e.matmul(
                        psS[:, :],
                        kst16[rbase:rbase + 64, 2048 * (h // 2) + 128 * j:2048 * (h // 2) + 128 * j + 128],
                        q16[rbase:rbase + 64, 512 * (h // 2):512 * (h // 2) + 512],
                        start=True, stop=False))
                    EM.op("t", [], lambda e, psS=psS, h=h, j=j, rb32=rb32: e.matmul(
                        psS[:, :],
                        kr2st16[rb32:rb32 + 32, 128 * j:128 * j + 128],
                        qr16[rb32:rb32 + 32, 512 * (h // 2):512 * (h // 2) + 512],
                        start=False, stop=True), inc=cT)
                    EM.op("s", [W(cT), (cT, E_free[esl])], lambda e, psS=psS, esl=esl: e.activation(
                        E16[:, 512 * esl:512 * esl + 512], psS[:, :], AF.Exp,
                        bias=-CEXP, scale=ISQ), inc=cS)
                    sS_free[ssl] = cS.n
                    dd = j - 4 * m
                    if dd >= 0:
                        off = 384 - 128 * dd
                        EM.op("v", [W(cS)], lambda e, esl=esl, off=off: e.tensor_tensor(
                            E16[:, 512 * esl:512 * esl + 512], E16[:, 512 * esl:512 * esl + 512],
                            TRI16[:, off:off + 512], ALU.mult), inc=cV)
                        ew = [W(cV)]
                    else:
                        ew = [W(cS)]
                    EM.op("t", ew, lambda e, psO=psO, b=b, h=h, j=j, esl=esl, nkt=nkt: e.matmul(
                        psO[b:b + 64, :], vst16[:, 1024 * j + 64 * h:1024 * j + 64 * h + 64],
                        E16[:, 512 * esl:512 * esl + 512],
                        start=(j == 0), stop=(j == nkt - 1)))
                    EM.op("t", [], lambda e, psL=psL, j=j, esl=esl, nkt=nkt: e.matmul(
                        psL[0:1, :], ones16[:, 0:1], E16[:, 512 * esl:512 * esl + 512],
                        start=(j == 0), stop=(j == nkt - 1)), inc=cT)
                    E_free[esl] = cT.n
                frow = fpool[0:1, 512 * oslot:512 * oslot + 512]
                EM.op("v", [W(cT)], lambda e, psL=psL, frow=frow: e.reciprocal(frow, psL[0:1, :]), inc=cV)
                EM.op("t", [W(cV)], lambda e, psL=psL, b=b, frow=frow: e.matmul(
                    psL[b:b + 64, :], onesf[0:1, 0:64], frow, start=True, stop=True), inc=cT)
                EM.op("v", [W(cT)], lambda e, psL=psL, b=b: e.tensor_copy(
                    fpool[b:b + 64, 1024:1536], psL[b:b + 64, :]), inc=cV)
                L_free[oslot] = cV.n
                EM.op("v", [W(cV)], lambda e, psO=psO, b=b, hp=hp: e.tensor_tensor(
                    ypair16[b:b + 64, 512 * hp:512 * hp + 512], psO[b:b + 64, :],
                    fpool[b:b + 64, 1024:1536], ALU.mult), inc=cV)
                O_free[oslot] = cV.n
            cv_core = cV.n

            if dbg and s == 0 and m == 0:
                EM.op("y", [(cV, cv_core)], lambda e: e.dma_start(out=DBGE, in_=ypair16[:, :]), inc=cY, k=16)
            for u in range(4):
                for half in range(2):
                    slot = (u * 2 + half) % 2
                    ps = PS2[slot]
                    w2 = [(cV, cv_core), (cV, ps_free[slot]), (cS, sS_free[slot])]
                    for p in range(8):
                        EM.op("t", w2 if p == 0 else [], lambda e, ps=ps, u=u, half=half, p=p: e.matmul(
                            ps[:, :], ypair16[:, 512 * p + 128 * u:512 * p + 128 * u + 128],
                            Wo16[:, 1024 * p + 512 * half:1024 * p + 512 * half + 512],
                            start=(p == 0), stop=(p == 7)), inc=cT if p == 7 else None)
                    EM.op("v", [W(cT)], lambda e, ps=ps, u=u, half=half: e.tensor_tensor(
                        x2tm[:, 1024 * u + 512 * half:1024 * u + 512 * half + 512], ps[:, :],
                        x2tm[:, 1024 * u + 512 * half:1024 * u + 512 * half + 512], ALU.add), inc=cV)
                    ps_free[slot] = cV.n

            for u in range(4):
                uslot = u % 2
                ug = 16 * s + 4 * m + u
                EM.op("y", [W(cV)], lambda e, u=u, ug=ug: e.dma_start(
                    out=X2S[ug], in_=x2tm[:, 1024 * u:1024 * (u + 1)]), inc=cY, k=16)
                if xf0_dma_y[uslot] > 0:
                    EM.op("v", [(cY, xf0_dma_y[uslot])], lambda e: e.nop())
                ln_stats_apply(x2tm[:, 1024 * u:1024 * (u + 1)],
                               xf0st[:, 1024 * uslot:1024 * (uslot + 1)])
                EM.op("y", [W(cV)], lambda e, uslot=uslot, ug=ug: e.dma_start(
                    out=XF0S[ug], in_=xf0st[:, 1024 * uslot:1024 * (uslot + 1)]), inc=cY, k=16)
                xf0_dma_y[uslot] = cY.n
            x2tm_free = [(cY, cY.n), (cV, cV.n)]
            barrier("tvs")

    at.close()
    ao.close()

    ff = ExitStack()

    def fsb(name, shape, dt):
        return ff.enter_context(nc.sbuf_tensor(name, list(shape), dt))

    ring16 = fsb("ring16", (128, 4 * 8192), BF16)
    xf32 = fsb("xf32", (128, 8 * 1024), F32)
    xf16 = fsb("xf16", (128, 8 * 1024), BF16)
    acc32 = fsb("acc32", (128, 8 * 1024), F32)
    h1b = fsb("h1b", (128, 8 * 512), BF16)
    h2b = fsb("h2b", (128, 8 * 512), BF16)
    silu_sc = fsb("silu_sc", (128, 2 * 512), F32)
    wbc = fsb("wbc", (128, 1024), F32)
    w8_16 = fsb("w8_16", (8, 1024), BF16)
    gate_all = fsb("gate_all", (128, 64), F32)
    biased_all = fsb("biased_all", (128, 64), F32)
    zall = fsb("zall", (128, 64), F32)
    wall = fsb("wall", (128, 64), F32)
    selsc = fsb("selsc", (128, 8), F32)
    tsc = fsb("tsc", (128, 8), F32)
    usc = fsb("usc", (128, 8), F32)
    ez8 = fsb("ez8", (128, 8), F32)
    max8 = fsb("max8", (128, 8), F32)
    rstat = fsb("rstat", (128, 4), F32)
    ysq16 = fsb("ysq16", (128, 1024), BF16)
    rbc = fsb("rbc", (128, 1024), F32)
    tmst = fsb("tmst", (128, 2 * 1024), F32)
    outst = fsb("outst", (128, 2 * 1024), F32)
    rstdr = rbc[0:1, :]

    psE0 = ff.enter_context(nc.psum_tensor("psE0", [128, 512], F32))
    psE1 = ff.enter_context(nc.psum_tensor("psE1", [128, 512], F32))
    psW8 = ff.enter_context(nc.psum_tensor("psW8", [8, 1024], F32))
    psR = ff.enter_context(nc.psum_tensor("psR", [1, 1024], F32))
    psC = ff.enter_context(nc.psum_tensor("psC", [128, 1024], F32))
    PSE = [psE0, psE1]

    EM.op("t", [(cG, cG.n)], lambda e: e.nop(), inc=cT)
    cv_attn_end = cV.n
    ct_attn_end = cT.n
    cs_attn_end = cS.n
    EM.op("v", [], lambda e: e.memset(gate_all[:, :], -1.0e30))
    EM.op("v", [], lambda e: e.memset(biased_all[:, :], -1.0e30))
    EM.op("v", [], lambda e: e.memset(zall[:, :], -1.0e30), inc=cV)

    uls = list(range(22))
    sub_seq = []
    for (mu, mg, md) in [(0, 1, 2), (3, 1, 2)] + [(4 + 3 * e_, 5 + 3 * e_, 6 + 3 * e_) for e_ in range(NE)]:
        for tile in range(2):
            sub_seq += [(mu, "up"), (mg, "gate"), (md, "down")]
    last_use_idx = {}
    for bi, (mat, _) in enumerate(sub_seq):
        last_use_idx[mat] = bi

    ring_last_use = [ct_attn_end] * 4
    ring_loaded_y = [0, 0, 0, 0]

    def issue_load(li, git_tag):
        mat = uls[li]
        sl = li % 4
        EM.op("y", [(cT, ring_last_use[sl]), (cG, wc_done_at[mat]), (cV, cv_attn_end)],
              lambda e, sl=sl, mat=mat: e.dma_start(
                  out=ring16[:, 8192 * sl:8192 * (sl + 1)].rearrange("p (c o) -> p c o", c=8),
                  in_=WC16[mat].rearrange("(kb p) o -> p kb o", p=128)),
              inc=cY, k=16)
        ring_loaded_y[sl] = cY.n
        return sl

    tm_free = [cv_attn_end, cv_attn_end]
    out_free = [0, 0]
    psC_free = {"v": 0}
    cv_w8 = 0
    wbc_copy_v = 0

    for g in range(NGRP):
        sq = g // 2
        gt0 = GRP * (g % 2)
        for u in range(8):
            usl = u % 2
            ug = 16 * sq + 8 * (g % 2) + u
            EM.op("y", [(cV, tm_free[usl])], lambda e, ug=ug, usl=usl: e.dma_start(
                out=tmst[:, 1024 * usl:1024 * (usl + 1)], in_=XF0S[ug]), inc=cY, k=16)
            cy_u = cY.n
            for c in range(8):
                EM.op("t", [(cY, cy_u), (cV, psC_free["v"])] if c == 0 else [],
                      lambda e, usl=usl, c=c: e.transpose(
                          psC[:, 128 * c:128 * (c + 1)],
                          tmst[:, 1024 * usl + 128 * c:1024 * usl + 128 * c + 128],
                          I128f[:, :]), inc=cT if c == 7 else None)
            EM.op("v", [W(cT)], lambda e, u=u: e.tensor_copy(
                xf32.rearrange("p (c t) -> p c t", c=8)[:, :, 128 * u:128 * (u + 1)],
                psC.rearrange("p (c t) -> p c t", c=8)[:, :, :]), inc=cV)
            tm_free[usl] = cV.n
            psC_free["v"] = cV.n
        for c in range(8):
            EM.op("v", [], lambda e, c=c: e.tensor_copy(
                xf16[:, 1024 * c:1024 * (c + 1)], xf32[:, 1024 * c:1024 * (c + 1)]), inc=cV)
        cv_xf = cV.n
        if dbg and g == 0:
            EM.op("y", [(cV, cv_xf)], lambda e: e.dma_start(out=DBGL, in_=xf32[:, :]), inc=cY, k=16)

        for it in range(2):
            pse_free = {0: (cV, cV.n), 1: (cV, cV.n)}
            for u in range(8):
                ps = PSE[u % 2]
                w3 = [pse_free[u % 2], (cV, cv_xf)]
                for cb in range(8):
                    EM.op("t", w3 if cb == 0 else [], lambda e, ps=ps, u=u, cb=cb, it=it: e.matmul(
                        ps[:, 0:NE], xf16[:, 1024 * cb + 128 * u:1024 * cb + 128 * u + 128],
                        rW16[:, 48 * it + 6 * cb:48 * it + 6 * cb + 6],
                        start=(cb == 0), stop=False))
                EM.op("t", [], lambda e, ps=ps, it=it: e.matmul(
                    ps[:, 0:NE], ones16[0:1, 0:128], rbgb16[0:1, 6 * it:6 * it + 6],
                    start=False, stop=True), inc=cT)
                EM.op("v", [W(cT)], lambda e, ps=ps, u=u: e.tensor_copy(
                    gate_all[:, 8 * u:8 * u + 6], ps[:, 0:NE]), inc=cV)
                pse_free[u % 2] = (cV, cV.n)
                EM.op("v", [W(cV), W(cS)], lambda e, u=u, it=it: e.tensor_tensor(
                    biased_all[:, 8 * u:8 * u + 6], gate_all[:, 8 * u:8 * u + 6],
                    gbbc[:, 8 * it:8 * it + 6], ALU.add), inc=cV)
                EM.op("v", [W(cV)], lambda e, u=u: e.max(out=max8[:, :], in_=biased_all[:, 8 * u:8 * u + 8]), inc=cV)
                EM.op("v", [W(cV)], lambda e, u=u: e.tensor_single_scalar(
                    selsc[:, 0:6], biased_all[:, 8 * u:8 * u + 6], max8[:, 1:2], ALU.is_ge), inc=cV)
                EM.op("v", [W(cV)], lambda e, u=u: e.tensor_tensor(
                    tsc[:, 0:6], gate_all[:, 8 * u:8 * u + 6], selsc[:, 0:6], ALU.mult), inc=cV)
                EM.op("v", [W(cV)], lambda e: e.tensor_scalar(
                    usc[:, 0:6], selsc[:, 0:6], 1.0e30, 1.0e30, ALU.mult, ALU.subtract), inc=cV)
                EM.op("v", [W(cV)], lambda e, u=u: e.tensor_add(
                    zall[:, 8 * u:8 * u + 6], tsc[:, 0:6], usc[:, 0:6]), inc=cV)
                EM.op("v", [W(cV)], lambda e, u=u: e.reduce_max(
                    rstat[:, 0:1], zall[:, 8 * u:8 * u + 8], axis=AX.X, negate=True), inc=cV)
                EM.op("s", [W(cV)], lambda e, u=u: e.activation(
                    ez8[:, 0:8], zall[:, 8 * u:8 * u + 8], AF.Exp, bias=rstat[:, 0:1]), inc=cS)
                EM.op("v", [W(cS)], lambda e: e.reduce_sum(rstat[:, 1:2], ez8[:, 0:8], axis=AX.X), inc=cV)
                EM.op("v", [W(cV)], lambda e: e.reciprocal(rstat[:, 2:3], rstat[:, 1:2]), inc=cV)
                EM.op("v", [W(cV)], lambda e, u=u: e.tensor_single_scalar(
                    wall[:, 8 * u:8 * u + 8], ez8[:, 0:8], rstat[:, 2:3], ALU.mult), inc=cV)
            cv_router = cV.n
            if dbg and g == 0 and it == 0:
                EM.op("y", [(cV, cv_router)], lambda e: e.dma_start(out=DBGI, in_=wall[:, :]), inc=cY, k=16)
            for u in range(8):
                EM.op("t", [(cV, cv_router), (cV, cv_w8)] if u == 0 else [], lambda e, u=u: e.transpose(
                    psW8[0:8, 128 * u:128 * (u + 1)], wall[:, 8 * u:8 * u + 8], I128f[:, :]),
                    inc=cT if u == 7 else None)
            EM.op("v", [W(cT)], lambda e: e.tensor_copy(w8_16[0:8, :], psW8[0:8, :]), inc=cV)
            cv_w8 = cV.n

            li = [0]

            def pump_loads(bi):
                while li[0] < 22 and (li[0] < 4 or last_use_idx[uls[li[0] - 4]] <= bi):
                    issue_load(li[0], None)
                    li[0] += 1

            pump_loads(-1)

            h1_free_T = cT.n
            h2_free_T = cT.n
            acc_started = {0: False, 1: False}
            cs_up = cS.n
            cv_h2 = cV.n
            cv_gate_muls = cV.n
            silu_free = [cV.n, cV.n]
            entries = [("shA", 0, 1, 2), ("shB", 3, 1, 2)] + \
                      [(e_, 4 + 3 * e_, 5 + 3 * e_, 6 + 3 * e_) for e_ in range(NE)]
            bi = -1
            for (tag, mu, mg, md) in entries:
                is_exp = isinstance(tag, int)
                if is_exp:
                    for t2 in range(2):
                        EM.op("t", [(cV, cv_w8), (cV, wbc_copy_v), (cV, psC_free["v"])],
                              lambda e, tag=tag, t2=t2: e.matmul(
                            psC[:, 512 * t2:512 * t2 + 512], SEL16[0:8, 128 * tag:128 * tag + 128],
                            w8_16[0:8, 512 * t2:512 * t2 + 512], start=True, stop=True), inc=cT)
                        EM.op("v", [W(cT)], lambda e, t2=t2: e.tensor_copy(
                            wbc[:, 512 * t2:512 * t2 + 512], psC[:, 512 * t2:512 * t2 + 512]), inc=cV)
                    wbc_copy_v = cV.n
                    psC_free["v"] = cV.n
                for tile in range(2):
                    for (mat, kind) in [(mu, "up"), (mg, "gate"), (md, "down")]:
                        bi += 1
                        sl = (uls.index(mat)) % 4
                        ly = ring_loaded_y[sl]
                        for co in range(8):
                            slot2 = co % 2
                            ps = PSE[slot2]
                            w4 = [pse_free[slot2]]
                            if co < 2:
                                w4.append((cY, ly))
                                if kind == "up":
                                    w4 += [(cT, h1_free_T), (cV, cv_xf)]
                                elif kind == "gate":
                                    w4 += [(cS, cs_up)]
                                else:
                                    w4 += [(cT, h2_free_T), (cV, cv_h2)]
                            for ci in range(8):
                                def bmm(e, ps=ps, sl=sl, co=co, ci=ci, tile=tile, kind=kind):
                                    if kind == "up":
                                        rhs = xf16[:, 1024 * ci + 512 * tile:1024 * ci + 512 * tile + 512]
                                    elif kind == "gate":
                                        rhs = h1b[:, 512 * ci:512 * ci + 512]
                                    else:
                                        rhs = h2b[:, 512 * ci:512 * ci + 512]
                                    return e.matmul(
                                        ps[:, :], ring16[:, 8192 * sl + 1024 * ci + 128 * co:
                                                         8192 * sl + 1024 * ci + 128 * co + 128],
                                        rhs, start=(ci == 0), stop=(ci == 7))
                                EM.op("t", w4 if ci == 0 else [], bmm, inc=cT if ci == 7 else None)
                            if kind == "up":
                                if is_exp:
                                    bcol = bup_st[:, 8 * tag + co:8 * tag + co + 1]
                                else:
                                    ns = 0 if tag == "shA" else 1
                                    bcol = sbup_st[:, 8 * ns + co:8 * ns + co + 1]
                                EM.op("s", [W(cT), (cV, cv_gate_muls)],
                                      lambda e, ps=ps, co=co, bcol=bcol: e.activation(
                                          h1b[:, 512 * co:512 * co + 512], ps[:, :], AF.Identity,
                                          bias=bcol), inc=cS)
                                pse_free[slot2] = (cS, cS.n)
                            elif kind == "gate":
                                bcol = (bg_st[:, 8 * tag + co:8 * tag + co + 1] if is_exp
                                        else sbg_st[:, co:co + 1])
                                ssl2 = co % 2
                                EM.op("s", [W(cT), (cV, silu_free[ssl2])],
                                      lambda e, ps=ps, bcol=bcol, ssl2=ssl2: e.activation(
                                          silu_sc[:, 512 * ssl2:512 * ssl2 + 512], ps[:, :],
                                          AF.Silu, bias=bcol), inc=cS)
                                pse_free[slot2] = (cS, cS.n)
                                if is_exp:
                                    EM.op("v", [W(cS)], lambda e, co=co, ssl2=ssl2: e.tensor_tensor(
                                        silu_sc[:, 512 * ssl2:512 * ssl2 + 512],
                                        silu_sc[:, 512 * ssl2:512 * ssl2 + 512],
                                        h1b[:, 512 * co:512 * co + 512], ALU.mult), inc=cV)
                                    EM.op("v", [W(cV)], lambda e, co=co, tile=tile, ssl2=ssl2: e.tensor_tensor(
                                        h2b[:, 512 * co:512 * co + 512],
                                        silu_sc[:, 512 * ssl2:512 * ssl2 + 512],
                                        wbc[:, 512 * tile:512 * tile + 512], ALU.mult), inc=cV)
                                else:
                                    EM.op("v", [W(cS), W(cV)], lambda e, co=co, ssl2=ssl2: e.tensor_tensor(
                                        h2b[:, 512 * co:512 * co + 512],
                                        silu_sc[:, 512 * ssl2:512 * ssl2 + 512],
                                        h1b[:, 512 * co:512 * co + 512], ALU.mult), inc=cV)
                                silu_free[ssl2] = cV.n
                            else:
                                accsl = acc32[:, 1024 * co + 512 * tile:1024 * co + 512 * tile + 512]
                                if not acc_started[tile]:
                                    EM.op("v", [W(cT)], lambda e, ps=ps, accsl=accsl: e.tensor_copy(
                                        accsl, ps[:, :]), inc=cV)
                                else:
                                    EM.op("v", [W(cT)], lambda e, ps=ps, accsl=accsl: e.tensor_tensor(
                                        accsl, ps[:, :], accsl, ALU.add), inc=cV)
                                if is_exp:
                                    bcol = bdn_st[:, 8 * tag + co:8 * tag + co + 1]
                                    EM.op("v", [W(cV)], lambda e, accsl=accsl, bcol=bcol, tile=tile:
                                          e.scalar_tensor_tensor(
                                              accsl, wbc[:, 512 * tile:512 * tile + 512], bcol,
                                              accsl, ALU.mult, ALU.add), inc=cV)
                                else:
                                    bcol = sbdn_st[:, co:co + 1]
                                    EM.op("v", [W(cV)], lambda e, accsl=accsl, bcol=bcol:
                                          e.tensor_single_scalar(accsl, accsl, bcol, ALU.add), inc=cV)
                                pse_free[slot2] = (cV, cV.n)
                        ct_mm = cT.n
                        if kind == "up":
                            cs_up = cS.n
                        elif kind == "gate":
                            h1_free_T = ct_mm
                            cv_h2 = cV.n
                            cv_gate_muls = cV.n
                        else:
                            h2_free_T = ct_mm
                            acc_started[tile] = True
                        ring_last_use[sl] = cT.n
                        pump_loads(bi)
            cv_acc = cV.n
            if dbg and g == 0 and it == 0:
                EM.op("y", [(cV, cv_acc)], lambda e: e.dma_start(out=DBGJ, in_=acc32[:, :]), inc=cY, k=16)
                acc_dump_y = cY.n
                EM.op("v", [(cY, acc_dump_y)], lambda e: e.nop())

            for c in range(8):
                EM.op("v", [(cV, cv_acc)] if c == 0 else [W(cV)], lambda e, c=c: e.tensor_tensor(
                    acc32[:, 1024 * c:1024 * (c + 1)], xf32[:, 1024 * c:1024 * (c + 1)],
                    acc32[:, 1024 * c:1024 * (c + 1)], ALU.add), inc=cV)
            cv_y = cV.n
            st_free = cT.n
            for c in range(8):
                EM.op("s", [(cV, cv_y), (cT, st_free)], lambda e, c=c: e.activation(
                    ysq16[:, :], acc32[:, 1024 * c:1024 * (c + 1)], AF.Square), inc=cS)
                for half in range(2):
                    EM.op("t", [W(cS)], lambda e, c=c, half=half: e.matmul(
                        psR[0:1, 512 * half:512 * half + 512], ones16[:, 0:1],
                        ysq16[:, 512 * half:512 * half + 512],
                        start=(c == 0), stop=(c == 7)), inc=cT)
                st_free = cT.n
            EM.op("s", [W(cT)], lambda e: e.activation(
                rstdr, psR[0:1, :], AF.Ln, bias=1e-6, scale=1.0 / 1024.0), inc=cS)
            EM.op("s", [], lambda e: e.activation(
                rstdr, rstdr, AF.Exp, scale=-0.5), inc=cS)
            for half in range(2):
                EM.op("t", [W(cS), (cV, psC_free["v"])] if half == 0 else [W(cS)],
                      lambda e, half=half: e.matmul(
                          psC[:, 512 * half:512 * half + 512], onesf[0:1, 0:128],
                          rstdr[0:1, 512 * half:512 * half + 512], start=True, stop=True), inc=cT)
            EM.op("v", [W(cT)], lambda e: e.tensor_copy(rbc[:, :], psC[:, :]), inc=cV)
            psC_free["v"] = cV.n
            for c in range(8):
                EM.op("v", [W(cV)], lambda e, c=c, it=it: e.scalar_tensor_tensor(
                    xf32[:, 1024 * c:1024 * (c + 1)], acc32[:, 1024 * c:1024 * (c + 1)],
                    nw_st[:, 8 * it + c:8 * it + c + 1], rbc[:, :], ALU.mult, ALU.mult), inc=cV)
            for c in range(8):
                EM.op("v", [W(cV)], lambda e, c=c: e.tensor_copy(
                    xf16[:, 1024 * c:1024 * (c + 1)], xf32[:, 1024 * c:1024 * (c + 1)]), inc=cV)
            cv_xf = cV.n
            if dbg and g == 0 and it == 0:
                EM.op("y", [(cV, cv_xf)], lambda e: e.dma_start(out=DBGK, in_=xf32[:, :]), inc=cY, k=16)
            barrier("tvs")

        for u in range(8):
            usl = u % 2
            ug = 16 * sq + 8 * (g % 2) + u
            EM.op("y", [(cV, tm_free[usl])], lambda e, ug=ug, usl=usl: e.dma_start(
                out=tmst[:, 1024 * usl:1024 * (usl + 1)], in_=X2S[ug]), inc=cY, k=16)
            cy_u = cY.n
            for c in range(8):
                EM.op("t", [(cV, cv_xf), (cV, psC_free["v"])] if c == 0 else [],
                      lambda e, u=u, c=c: e.transpose(
                          psC[:, 128 * c:128 * (c + 1)],
                          xf32[:, 1024 * c + 128 * u:1024 * c + 128 * u + 128],
                          I128f[:, :]), inc=cT if c == 7 else None)
            for half in range(2):
                EM.op("v", [W(cT), (cY, cy_u), (cY, out_free[usl])], lambda e, usl=usl, half=half:
                      e.tensor_tensor(
                          outst[:, 1024 * usl + 512 * half:1024 * usl + 512 * half + 512],
                          psC[:, 512 * half:512 * half + 512],
                          tmst[:, 1024 * usl + 512 * half:1024 * usl + 512 * half + 512],
                          ALU.add), inc=cV)
            tm_free[usl] = cV.n
            psC_free["v"] = cV.n
            EM.op("y", [W(cV)], lambda e, usl=usl, sq=sq, gt0=gt0, u=u: e.dma_start(
                out=OUT[sq, gt0 + 128 * u:gt0 + 128 * u + 128, :],
                in_=outst[:, 1024 * usl:1024 * (usl + 1)]), inc=cY, k=16)
            out_free[usl] = cY.n
        barrier("tvs")

    ff.close()

    EM.op("y", [W(cY), W(cG)], lambda e: e.nop())

    blk = ctx.enter_context(nc.Block())
    eng_map = {"t": blk.tensor, "v": blk.vector, "s": blk.scalar,
               "y": blk.sync, "g": blk.gpsimd}
    for key in "gytvs":
        opl = EM.ops[key]

        def body(eng, opl=opl):
            for f in opl:
                f(eng)
        eng_map[key](body)

    return nc


_NC_CACHE = {}


def _get_nc():
    if "nc" not in _NC_CACHE:
        _NC_CACHE["nc"] = build_program()
    return _NC_CACHE["nc"]


def kernel(**inputs):
    x = np.ascontiguousarray(np.asarray(inputs["L0_attn_x"], dtype=np.float32))

    W_KR = np.asarray(inputs["L0_attn_W_KR"], dtype=np.float32)
    W_KR2 = np.ascontiguousarray(np.concatenate([W_KR, W_KR], axis=1))
    W_O = np.asarray(inputs["L0_attn_W_O"], dtype=np.float32)
    A = W_O.reshape(16, 64, 1024)
    W_O_pair = np.ascontiguousarray(
        np.concatenate([A[0:8], A[8:16]], axis=1).transpose(1, 0, 2))

    scale = _rope_scale()
    SC2 = np.ascontiguousarray(np.tile(scale.T, (2, 1)))
    p = np.arange(128)[:, None]
    xcol = np.arange(896)[None, :]
    TRI = (p <= xcol - 384).astype(np.float32)
    SEL = np.zeros((8, 1024), dtype=np.float32)
    for e_ in range(8):
        SEL[e_, 128 * e_:128 * (e_ + 1)] = 1.0
    I128 = np.eye(128, dtype=np.float32)

    def c32(a):
        return np.ascontiguousarray(np.asarray(a, dtype=np.float32))

    common = {
        "W_DQ": c32(inputs["L0_attn_W_DQ"]),
        "W_UQ": c32(inputs["L0_attn_W_UQ"]),
        "W_QR": c32(inputs["L0_attn_W_QR"]),
        "W_DKV": c32(inputs["L0_attn_W_DKV"]),
        "W_UK": c32(inputs["L0_attn_W_UK"]),
        "W_UV": c32(inputs["L0_attn_W_UV"]),
        "W_KR2": W_KR2,
        "W_O_pair": W_O_pair,
        "router_W": c32(inputs["L0_ffn_router_W"]),
        "router_b": c32(inputs["L0_ffn_router_b"]),
        "gate_bias": c32(inputs["L0_ffn_gate_bias"]),
        "exp_Wup": c32(inputs["L0_ffn_exp_Wup"]),
        "exp_bup": c32(inputs["L0_ffn_exp_bup"]),
        "exp_Wg": c32(inputs["L0_ffn_exp_Wg"]),
        "exp_bg": c32(inputs["L0_ffn_exp_bg"]),
        "exp_Wdown": c32(inputs["L0_ffn_exp_Wdown"]),
        "exp_bdown": c32(inputs["L0_ffn_exp_bdown"]),
        "share_Wup": c32(inputs["L0_ffn_share_Wup"]),
        "share_bup": c32(inputs["L0_ffn_share_bup"]),
        "share_Wg": c32(inputs["L0_ffn_share_Wg"]),
        "share_bg": c32(inputs["L0_ffn_share_bg"]),
        "share_Wdown": c32(inputs["L0_ffn_share_Wdown"]),
        "share_bdown": c32(inputs["L0_ffn_share_bdown"]),
        "norm_w": c32(inputs["L0_ffn_norm_w"]),
        "CONST_I": I128,
        "CONST_TRI": TRI,
        "CONST_SC2": SC2,
        "CONST_SEL": SEL,
    }

    nc = _get_nc()
    in_maps = []
    for c in range(NCORES):
        m = dict(common)
        m["x"] = np.ascontiguousarray(x[NSEQ * c:NSEQ * (c + 1)])
        in_maps.append(m)
    res = run_bass_kernel_spmd(nc, in_maps, core_ids=list(range(NCORES)))
    out = np.concatenate([np.asarray(r["out"]) for r in res.results], axis=0)
    return out
